# version 02
# baseline (speedup 1.0000x reference)
import os
import numpy as np
from contextlib import ExitStack

import concourse.bass as bass
import concourse.mybir as mybir
import bass_rust
from concourse.bass_utils import run_bass_kernel_spmd

F32 = mybir.dt.float32
BF16 = mybir.dt.bfloat16
ALU = mybir.AluOpType
ACT = mybir.ActivationFunctionType

D = 1024
E = 2048
NST = 16
RNK = 64
HALF = 4096
HALO = 3
DH = 4096
NCORES = 8

NET = 506
NT2 = 9
T3 = 1024
NT3 = HALF // T3

RND_MAGIC = 12582912.0

PHASES = tuple(int(c) for c in os.environ.get("MK_PHASES", "234"))
DEBUG_SPILLS = bool(int(os.environ.get("MK_DEBUG", "0")))


def _row0(i):
    return i * NET if i < NT2 - 1 else (HALF + 2 * HALO) - 512


def _net_cols(i):
    r0 = _row0(i)
    t_lo = i * NET
    t_hi = min(t_lo + NET, HALF)
    c_lo = t_lo - r0 + HALO
    c_hi = t_hi - r0 + HALO
    return c_lo, c_hi, t_lo, t_hi


class Sem:

    def __init__(self, ctx, nc, name):
        self.h = ctx.enter_context(nc.semaphore(name))
        self.n = 0

    def inc(self, instr, k=1):
        instr.then_inc(self.h, k)
        self.n += k
        return instr

    def wait_all(self, eng):
        if self.n > 0:
            eng.wait_ge(self.h, self.n)


def build_nc():
    nc = bass.Bass("TRN2", target_bir_lowering=False, debug=False)

    x_pad = nc.dram_tensor("x_pad", [HALF + 2 * HALO, D], F32, kind="ExternalInput").ap()
    win = {}
    for d in ("a", "b"):
        win[d + "_inw"] = nc.dram_tensor(f"w{d}_inw", [D, 2 * E], F32, kind="ExternalInput").ap()
        win[d + "_conv"] = nc.dram_tensor(f"w{d}_conv", [E, 4], F32, kind="ExternalInput").ap()
        win[d + "_convb"] = nc.dram_tensor(f"w{d}_convb", [E], F32, kind="ExternalInput").ap()
        win[d + "_xproj"] = nc.dram_tensor(f"w{d}_xproj", [E, RNK + 2 * NST], F32, kind="ExternalInput").ap()
        win[d + "_dtw"] = nc.dram_tensor(f"w{d}_dtw", [RNK, E], F32, kind="ExternalInput").ap()
        win[d + "_dtb"] = nc.dram_tensor(f"w{d}_dtb", [E], F32, kind="ExternalInput").ap()
        win[d + "_alog"] = nc.dram_tensor(f"w{d}_alog", [E, NST], F32, kind="ExternalInput").ap()
        win[d + "_dp"] = nc.dram_tensor(f"w{d}_dp", [E], F32, kind="ExternalInput").ap()
        win[d + "_outw"] = nc.dram_tensor(f"w{d}_outw", [E, D], F32, kind="ExternalInput").ap()
    ident_in = nc.dram_tensor("ident_in", [128, 128], F32, kind="ExternalInput").ap()
    sel_in = nc.dram_tensor("sel_in", [64, 32 * 128], BF16, kind="ExternalInput").ap()
    w1t = nc.dram_tensor("w1t", [D, DH], F32, kind="ExternalInput").ap()
    w2t = nc.dram_tensor("w2t", [DH, D], F32, kind="ExternalInput").ap()
    fcw = nc.dram_tensor("fcw", [DH, 3], F32, kind="ExternalInput").ap()
    fcb = nc.dram_tensor("fcb", [DH], F32, kind="ExternalInput").ap()

    out_half = nc.dram_tensor("out_half", [HALF, D], F32, kind="ExternalOutput").ap()

    spill_kind = {"kind": "ExternalOutput"} if DEBUG_SPILLS else {}

    def dspill(name, shape, dt=F32):
        return nc.dram_tensor(name, shape, dt, **spill_kind).ap()

    sp_xn = dspill("sp_xn", [8, 128, HALF + 2 * HALO])
    sp_ls = dspill("sp_ls", [2, 16, 128, HALF])
    sp_u = dspill("sp_u", [2, 16, 128, HALF])
    sp_sz = dspill("sp_sz", [2, 16, 128, HALF])
    sp_pr = dspill("sp_pr", [2, 2 * NST, HALF])
    sp_y = dspill("sp_y", [2, 16, 128, HALF], BF16)
    sp_oat = dspill("sp_oat", [8, 128, HALF])
    sp_q2 = dspill("sp_q2", [32, 128, HALF], BF16)
    sp_s2 = dspill("sp_s2", [1, HALF])
    sp_mrow = nc.dram_tensor("sp_mrow", [1, 128, 4], F32).ap()
    sp_wstash = nc.dram_tensor("sp_wstash", [32, 128, 4096], F32).ap()
    cc_in = nc.dram_tensor("cc_in", [1, E * NST], F32).ap()
    cc_out = nc.dram_tensor("cc_out", [1, E * NST], F32).ap()

    ctx = ExitStack()
    with ctx:
        def sb(name, shape, dt=F32):
            return ctx.enter_context(nc.sbuf_tensor(name, shape, dt))

        ident = sb("ident", [128, 128], F32)
        ones_col = sb("ones_col", [128, 1], F32)
        ones_row = sb("ones_row", [1, 128], F32)
        ones_row_bf = sb("ones_row_bf", [1, 128], BF16)
        nA = sb("nA", [128, 2 * 256])
        negdtb = sb("negdtb", [128, 2 * 16])
        convb_sb = sb("convb_sb", [128, 2 * 16])
        convw_sb = sb("convw_sb", [128, 2 * 64])
        dp_sb = sb("dp_sb", [128, 2 * 16])
        fcb_sb = sb("fcb_sb", [128, 32])
        fcw_sb = sb("fcw_sb", [128, 96])
        carry = [sb(f"carry{j}", [128, 256]) for j in range(2)]
        sel = sb("sel", [64, 32 * 128], BF16)
        carryA = sb("carryA", [128, 256])
        eps_b = sb("eps_b", [128, 1], F32)
        onec_b = sb("onec_b", [128, 1], F32)

        s_ld = Sem(ctx, nc, "s_ld")
        s_lw = Sem(ctx, nc, "s_lw")
        s_lx = Sem(ctx, nc, "s_lx")
        s_st = Sem(ctx, nc, "s_st")
        s_v = Sem(ctx, nc, "s_v")
        s_s = Sem(ctx, nc, "s_s")
        s_t = Sem(ctx, nc, "s_t")
        s_g = Sem(ctx, nc, "s_g")
        s_cc = Sem(ctx, nc, "s_cc")
        s_cd = Sem(ctx, nc, "s_cd")

        block = ctx.enter_context(nc.Block())

        def on(engine_attr):
            def run(f):
                engine_attr(f)
            return run

        @block.gpsimd
        def _(g):
            s_g.inc(g.memset(ones_col[:], 1.0), 1)
            s_g.inc(g.memset(eps_b[:], 1e-5), 1)
            s_g.inc(g.memset(ones_row_bf[:], 1.0), 1)
            s_g.inc(g.memset(onec_b[:], 1.0), 1)
            s_g.inc(g.memset(ones_row[:], 1.0), 1)

        @block.sync
        def _(sy):
            s_ld.inc(sy.dma_start(out=ident[:], in_=ident_in), 16)
            s_ld.inc(sy.dma_start(out=sel[:], in_=sel_in), 16)
            for di, d in enumerate(("a", "b")):
                s_ld.inc(sy.dma_start(out=nA[:, di * 256:(di + 1) * 256].rearrange("p (e n) -> p e n", e=16),
                                      in_=win[d + "_alog"].rearrange("(e p) n -> p e n", p=128)), 16)
                s_ld.inc(sy.dma_start(out=negdtb[:, di * 16:(di + 1) * 16],
                                      in_=win[d + "_dtb"].rearrange("(e p) -> p e", p=128), allow_slow_non_contiguous=True), 16)
                s_ld.inc(sy.dma_start(out=convb_sb[:, di * 16:(di + 1) * 16],
                                      in_=win[d + "_convb"].rearrange("(e p) -> p e", p=128), allow_slow_non_contiguous=True), 16)
                s_ld.inc(sy.dma_start(out=convw_sb[:, di * 64:(di + 1) * 64].rearrange("p (e k) -> p e k", e=16),
                                      in_=win[d + "_conv"].rearrange("(e p) k -> p e k", p=128)), 16)
                s_ld.inc(sy.dma_start(out=dp_sb[:, di * 16:(di + 1) * 16],
                                      in_=win[d + "_dp"].rearrange("(e p) -> p e", p=128), allow_slow_non_contiguous=True), 16)
            s_ld.inc(sy.dma_start(out=fcb_sb[:], in_=fcb.rearrange("(h p) -> p h", p=128), allow_slow_non_contiguous=True), 16)
            s_ld.inc(sy.dma_start(out=fcw_sb[:].rearrange("p (h k) -> p h k", h=32),
                                  in_=fcw.rearrange("(h p) k -> p h k", p=128)), 16)

        @block.scalar
        def _(s):
            s_ld.wait_all(s)
            s_s.inc(s.activation(nA[:], nA[:], ACT.Exp, bias=0.0, scale=1.0), 1)

        @block.vector
        def _(v):
            s_ld.wait_all(v)
            s_v.inc(v.tensor_scalar(negdtb[:], negdtb[:], -1.0, 0.0, ALU.mult, ALU.add), 1)

        if 2 in PHASES:
            p2 = ExitStack()
            with p2:
                def sb2(name, shape, dt=F32):
                    return p2.enter_context(nc.sbuf_tensor(name, shape, dt))

                xtok = sb2("xtok", [128, 4 * D])
                xfm = sb2("xfm", [128, 8 * 512])
                xnf = sb2("xnf", [128, 8 * 512])
                xnb = sb2("xnb", [128, 8 * 512], BF16)
                wib = sb2("wib", [128, 8 * 4096], BF16)
                wxp = sb2("wxp", [128, 16 * 96], BF16)
                wdt = sb2("wdt", [64, 16 * 128], BF16)
                xi_scr = sb2("xi_scr", [128, 2 * 512])
                sz_scr = sb2("sz_scr", [128, 2 * 512])
                cv_scr = sb2("cv_scr", [128, 2 * 512])
                u_sb = sb2("u_sb", [128, 16 * NET])
                ub_sb = sb2("ub_sb", [128, 16 * NET], BF16)
                dt16 = sb2("dt16", [64, NET], BF16)
                bc_sb = sb2("bc_sb", [32, NET])
                strow = xtok
                sig_sb = cv_scr
                MEAN, VAR, TMP = 0, 512, 1024
                RQ, MR = VAR, TMP
                wld = xfm

                ps_tp = p2.enter_context(nc.psum_tensor("ps_tp", [128, 512], F32))
                ps_sx = p2.enter_context(nc.psum_tensor("ps_sx", [128, 512], F32))
                ps_sq = p2.enter_context(nc.psum_tensor("ps_sq", [128, 512], F32))
                ps_mm = [p2.enter_context(nc.psum_tensor(f"ps_mm{j}", [128, 512], F32)) for j in range(2)]
                ps_xp = p2.enter_context(nc.psum_tensor("ps_xp", [128, 512], F32))

                for di, d in enumerate(("a", "b")):
                    for r in range(8):
                        @block.sync
                        def _(sy, r=r, d=d):
                            s_s.wait_all(sy)
                            s_t.wait_all(sy)
                            s_lw.inc(sy.dma_start(out=wld[:, :4096],
                                                  in_=win[d + "_inw"][r * 128:(r + 1) * 128, :]), 16)

                        @block.scalar
                        def _(s, r=r):
                            s_lw.wait_all(s)
                            s_s.inc(s.copy(wib[:, r * 4096:(r + 1) * 4096], wld[:, :4096]), 1)

                    @block.sync
                    def _(sy, d=d):
                        s_s.wait_all(sy)
                        s_lw.inc(sy.dma_start(out=wld[:, :96 * 16].rearrange("p (e j) -> p e j", e=16),
                                              in_=win[d + "_xproj"].rearrange("(e p) j -> p e j", p=128)), 16)
                        s_lw.inc(sy.dma_start(out=wld[0:64, 96 * 16:96 * 16 + 16 * 128],
                                              in_=win[d + "_dtw"]), 16)

                    @block.scalar
                    def _(s):
                        s_lw.wait_all(s)
                        s_s.inc(s.copy(wxp[:], wld[:, :96 * 16]), 1)
                        s_s.inc(s.copy(wdt[:], wld[0:64, 96 * 16:96 * 16 + 16 * 128]), 1)

                    for i in range(NT2):
                        r0 = _row0(i)
                        c_lo, c_hi, t_lo, t_hi = _net_cols(i)
                        nw = c_hi - c_lo

                        @block.vector
                        def _(v):
                            s_st.wait_all(v)

                        @block.scalar
                        def _(s):
                            s_st.wait_all(s)

                        @block.tensor
                        def _(t):
                            s_st.wait_all(t)

                        if di == 0:
                            @block.sync
                            def _(sy, r0=r0):
                                s_t.wait_all(sy)
                                for bj in range(4):
                                    s_lx.inc(sy.dma_start(out=xtok[:, bj * D:(bj + 1) * D],
                                                          in_=x_pad[r0 + bj * 128:r0 + (bj + 1) * 128, :]), 16)

                            for db in range(8):
                                @block.tensor
                                def _(t, db=db):
                                    s_lx.wait_all(t)
                                    s_s.wait_all(t)
                                    s_v.wait_all(t)
                                    s_ld.wait_all(t)
                                    for tb in range(4):
                                        mm = t.transpose(ps_tp[:, tb * 128:(tb + 1) * 128],
                                                         xtok[:, tb * D + db * 128: tb * D + (db + 1) * 128],
                                                         ident[:])
                                    s_t.inc(mm, 1)

                                @block.scalar
                                def _(s, db=db):
                                    s_t.wait_all(s)
                                    s_s.inc(s.copy(xfm[:, db * 512:(db + 1) * 512], ps_tp[:, :]), 1)

                            for db in range(8):
                                @block.scalar
                                def _(s, db=db):
                                    s_t.wait_all(s)
                                    s_s.inc(s.activation(cv_scr[:, (db % 2) * 512:(db % 2 + 1) * 512],
                                                         xfm[:, db * 512:(db + 1) * 512],
                                                         ACT.Square, bias=0.0, scale=1.0), 1)

                                @block.tensor
                                def _(t, db=db):
                                    s_s.wait_all(t)
                                    s_g.wait_all(t)
                                    t.matmul(ps_sx[0:1, :], ones_col[:], xfm[:, db * 512:(db + 1) * 512],
                                             start=(db == 0), stop=(db == 7))
                                    mm = t.matmul(ps_sq[0:1, :], ones_col[:],
                                                  cv_scr[:, (db % 2) * 512:(db % 2 + 1) * 512],
                                                  start=(db == 0), stop=(db == 7))
                                    s_t.inc(mm, 1)

                            @block.vector
                            def _(v):
                                s_t.wait_all(v)
                                v.tensor_scalar(strow[0:1, MEAN:MEAN + 512], ps_sx[0:1, :], 1.0 / D, 0.0, ALU.mult, ALU.add)
                                v.tensor_scalar(strow[0:1, VAR:VAR + 512], ps_sq[0:1, :], 1.0 / D, 0.0, ALU.mult, ALU.add)
                                v.drain()
                                v.tensor_tensor(strow[0:1, TMP:TMP + 512], strow[0:1, MEAN:MEAN + 512],
                                                strow[0:1, MEAN:MEAN + 512], ALU.mult)
                                v.drain()
                                s_v.inc(v.tensor_tensor(strow[0:1, VAR:VAR + 512], strow[0:1, VAR:VAR + 512],
                                                        strow[0:1, TMP:TMP + 512], ALU.subtract), 1)

                            @block.scalar
                            def _(s):
                                s_v.wait_all(s)
                                s_s.inc(s.activation(strow[0:1, TMP:TMP + 512], strow[0:1, VAR:VAR + 512],
                                                     ACT.Sqrt, bias=eps_b[0:1, 0:1], scale=1.0), 1)

                            @block.vector
                            def _(v):
                                s_s.wait_all(v)
                                v.reciprocal(strow[0:1, RQ:RQ + 512], strow[0:1, TMP:TMP + 512])
                                v.drain()
                                s_v.inc(v.tensor_tensor(strow[0:1, MR:MR + 512], strow[0:1, MEAN:MEAN + 512],
                                                        strow[0:1, RQ:RQ + 512], ALU.mult), 1)

                            @block.tensor
                            def _(t):
                                s_v.wait_all(t)
                                s_g.wait_all(t)
                                t.matmul(ps_sx[:, :], ones_row[:], strow[0:1, RQ:RQ + 512], start=True, stop=True)
                                mm = t.matmul(ps_sq[:, :], ones_row[:], strow[0:1, MR:MR + 512], start=True, stop=True)
                                s_t.inc(mm, 1)

                            @block.vector
                            def _(v):
                                s_t.wait_all(v)
                                for db in range(8):
                                    v.tensor_tensor(xnf[:, db * 512:(db + 1) * 512],
                                                    xfm[:, db * 512:(db + 1) * 512], ps_sx[:, :], ALU.mult)
                                    v.drain()
                                    s_v.inc(v.tensor_tensor(xnf[:, db * 512:(db + 1) * 512],
                                                            xnf[:, db * 512:(db + 1) * 512], ps_sq[:, :],
                                                            ALU.subtract), 1)

                            @block.scalar
                            def _(s):
                                s_v.wait_all(s)
                                s_s.inc(s.copy(xnb[:], xnf[:]), 1)

                            @block.gpsimd
                            def _(g, r0=r0):
                                s_s.wait_all(g)
                                for db in range(8):
                                    s_st.inc(g.dma_start(out=sp_xn[db, :, r0:r0 + 512],
                                                         in_=xnf[:, db * 512:(db + 1) * 512]), 16)
                        else:
                            @block.sync
                            def _(sy, r0=r0):
                                s_s.wait_all(sy)
                                s_t.wait_all(sy)
                                for db in range(8):
                                    s_lx.inc(sy.dma_start(out=xnf[:, db * 512:(db + 1) * 512],
                                                          in_=sp_xn[db, :, r0:r0 + 512]), 16)

                            @block.scalar
                            def _(s):
                                s_lx.wait_all(s)
                                s_s.inc(s.copy(xnb[:], xnf[:]), 1)

                        for m in range(32):
                            pm = ps_mm[m % 2]
                            pp = m % 2

                            @block.tensor
                            def _(t, m=m, pm=pm):
                                s_s.wait_all(t)
                                s_v.wait_all(t)
                                for k in range(8):
                                    mm = t.matmul(pm[:, :], wib[:, k * 4096 + m * 128: k * 4096 + (m + 1) * 128],
                                                  xnb[:, k * 512:(k + 1) * 512], start=(k == 0), stop=(k == 7))
                                s_t.inc(mm, 1)

                            if m < 16:
                                et = m

                                @block.scalar
                                def _(s, pm=pm, pp=pp):
                                    s_t.wait_all(s)
                                    s_s.inc(s.copy(xi_scr[:, pp * 512:(pp + 1) * 512], pm[:, :]), 1)

                                @block.vector
                                def _(v, et=et, di=di, c_lo=c_lo, nw=nw, pp=pp):
                                    s_s.wait_all(v)
                                    def src(k):
                                        if di == 0:
                                            return pp * 512 + (c_lo - HALO) + k
                                        return pp * 512 + (c_lo + HALO) - k
                                    cw = lambda k: convw_sb[:, di * 64 + et * 4 + k: di * 64 + et * 4 + k + 1]
                                    v.tensor_scalar(cv_scr[:, pp * 512:pp * 512 + nw],
                                                    xi_scr[:, src(0):src(0) + nw], cw(0), 0.0,
                                                    ALU.mult, ALU.add)
                                    for k in range(1, 4):
                                        v.drain()
                                        vi = v.scalar_tensor_tensor(cv_scr[:, pp * 512:pp * 512 + nw],
                                                                    xi_scr[:, src(k):src(k) + nw], cw(k),
                                                                    cv_scr[:, pp * 512:pp * 512 + nw],
                                                                    ALU.mult, ALU.add)
                                    s_v.inc(vi, 1)

                                @block.scalar
                                def _(s, et=et, di=di, nw=nw, pp=pp):
                                    s_v.wait_all(s)
                                    s_s.inc(s.activation(u_sb[:, et * NET:et * NET + nw],
                                                         cv_scr[:, pp * 512:pp * 512 + nw], ACT.Silu,
                                                         bias=convb_sb[:, di * 16 + et: di * 16 + et + 1],
                                                         scale=1.0), 1)

                                @block.vector
                                def _(v, et=et, nw=nw):
                                    s_s.wait_all(v)
                                    s_v.inc(v.tensor_copy(ub_sb[:, et * NET:et * NET + nw],
                                                          u_sb[:, et * NET:et * NET + nw]), 1)
                            else:
                                et = m - 16

                                @block.scalar
                                def _(s, pm=pm, pp=pp):
                                    s_t.wait_all(s)
                                    s_st.wait_all(s)
                                    s_s.inc(s.activation(sz_scr[:, pp * 512:(pp + 1) * 512], pm[:, :],
                                                         ACT.Silu, bias=0.0, scale=1.0), 1)

                                @block.gpsimd
                                def _(g, et=et, di=di, c_lo=c_lo, nw=nw, t_lo=t_lo, pp=pp):
                                    s_s.wait_all(g)
                                    s_st.inc(g.dma_start(out=sp_sz[di, et, :, t_lo:t_lo + nw],
                                                         in_=sz_scr[:, pp * 512 + c_lo: pp * 512 + c_lo + nw]), 16)

                        @block.gpsimd
                        def _(g, di=di, t_lo=t_lo, nw=nw):
                            s_s.wait_all(g)
                            s_v.wait_all(g)
                            for et in range(16):
                                s_st.inc(g.dma_start(out=sp_u[di, et, :, t_lo:t_lo + nw],
                                                     in_=u_sb[:, et * NET:et * NET + nw]), 16)

                        @block.tensor
                        def _(t, nw=nw):
                            s_v.wait_all(t)
                            for k in range(16):
                                mm = t.matmul(ps_xp[0:96, :nw], wxp[:, k * 96:(k + 1) * 96],
                                              ub_sb[:, k * NET:k * NET + nw], start=(k == 0), stop=(k == 15))
                            s_t.inc(mm, 1)

                        @block.scalar
                        def _(s, nw=nw):
                            s_t.wait_all(s)
                            s_s.inc(s.copy(dt16[:, :nw], ps_xp[0:64, :nw]), 1)

                        @block.vector
                        def _(v, nw=nw):
                            s_t.wait_all(v)
                            s_v.inc(v.tensor_copy(bc_sb[:, :nw], ps_xp[64:96, :nw]), 1)

                        @block.gpsimd
                        def _(g, di=di, t_lo=t_lo, nw=nw):
                            s_v.wait_all(g)
                            s_st.inc(g.dma_start(out=sp_pr[di, :, t_lo:t_lo + nw], in_=bc_sb[:, :nw]), 16)

                        for et in range(16):
                            @block.tensor
                            def _(t, et=et, nw=nw):
                                s_s.wait_all(t)
                                mm = t.matmul(ps_mm[0][:, :nw], wdt[:, et * 128:(et + 1) * 128], dt16[:, :nw],
                                              start=True, stop=True)
                                s_t.inc(mm, 1)

                            @block.scalar
                            def _(s, et=et, di=di, nw=nw):
                                s_t.wait_all(s)
                                s_st.wait_all(s)
                                s.activation(sig_sb[:, :nw], ps_mm[0][:, :nw], ACT.Sigmoid,
                                             bias=negdtb[:, di * 16 + et: di * 16 + et + 1], scale=-1.0)
                                s.drain()
                                s_s.inc(s.activation(sig_sb[:, :nw],
                                                     sig_sb[:, :nw], ACT.Ln, bias=0.0, scale=1.0), 1)

                            @block.gpsimd
                            def _(g, et=et, di=di, t_lo=t_lo, nw=nw):
                                s_s.wait_all(g)
                                s_st.inc(g.dma_start(out=sp_ls[di, et, :, t_lo:t_lo + nw],
                                                     in_=sig_sb[:, :nw]), 16)

        if 3 in PHASES:
            s_l3 = Sem(ctx, nc, "s_l3")
            p3 = ExitStack()
            with p3:
                def sb3(name, shape, dt=F32):
                    return p3.enter_context(nc.sbuf_tensor(name, shape, dt))

                ld_sb = sb3("ld_sb", [128, 16 * T3])
                u3_sb = sb3("u3_sb", [128, 16 * T3])
                sz3 = sb3("sz3", [128, 2 * T3])
                bc3 = sb3("bc3", [32, T3])
                bchl = sb3("bchl", [64, T3], BF16)
                bcs = sb3("bcs", [32, T3])
                a3 = [sb3(f"a3_{q}", [128, T3]) for q in range(2)]
                b3 = [sb3(f"b3_{q}", [128, T3]) for q in range(2)]
                h3 = [sb3(f"h3_{q}", [128, T3]) for q in range(2)]
                tmp3 = sb3("tmp3", [128, T3])
                w3 = sb3("w3", [128, T3])
                y3 = sb3("y3", [128, T3])
                yg3 = sb3("yg3", [128, 2 * T3], BF16)
                csum = sb3("csum", [128, 256])

                ps_b3 = [[p3.enter_context(nc.psum_tensor(f"ps_b3{q}_{h}", [128, 512], F32)) for h in range(2)]
                         for q in range(2)]
                ps_c3 = [[p3.enter_context(nc.psum_tensor(f"ps_c3{q}_{h}", [128, 512], F32)) for h in range(2)]
                         for q in range(2)]

                cons_a = [0, 0]
                cons_ps = [0, 0]
                prod_a = [0, 0]
                prod_ps = [0, 0]

                for ph in range(2):
                    di = ph
                    if ph == 1:
                        @block.gpsimd
                        def _(g):
                            s_v.wait_all(g)
                            s_cd.inc(g.dma_start(out=cc_in.rearrange("o (p n) -> (o p) n", p=128),
                                                 in_=carry[0][:, :]), 16)
                            g.wait_ge(s_cd.h, s_cd.n)
                            s_cc.inc(g.collective_compute(
                                "AllReduce", ALU.add,
                                replica_groups=[[0, 1], [2, 3], [4, 5], [6, 7]],
                                ins=[cc_in.opt()],
                                outs=[cc_out.opt()],
                            ), 1)
                            g.wait_ge(s_cc.h, s_cc.n)
                            s_cd.inc(g.dma_start(out=csum[:, :],
                                                 in_=cc_out.rearrange("o (p n) -> (o p) n", p=128)), 16)

                        @block.vector
                        def _(v):
                            s_cd.wait_all(v)
                            v.drain()
                            s_v.inc(v.tensor_tensor(carry[0][:, :], csum[:, :], carry[0][:, :], ALU.subtract), 1)

                    tiles = list(range(NT3)) if ph == 0 else list(range(NT3 - 1, -1, -1))
                    for jj, ti in enumerate(tiles):
                        t0 = ti * T3
                        qc_prev = jj % 2
                        qc_new = (jj + 1) % 2
                        rev = (ph == 1)

                        @block.sync
                        def _(sy, di=di, t0=t0):
                            s_v.wait_all(sy)
                            s_s.wait_all(sy)
                            s_t.wait_all(sy)
                            s_l3.inc(sy.dma_start(
                                out=ld_sb[:].rearrange("p (e t) -> p e t", e=16),
                                in_=sp_ls[di, :, :, t0:t0 + T3].rearrange("e p t -> p e t")), 16)
                            s_l3.inc(sy.dma_start(
                                out=u3_sb[:].rearrange("p (e t) -> p e t", e=16),
                                in_=sp_u[di, :, :, t0:t0 + T3].rearrange("e p t -> p e t")), 16)
                            s_l3.inc(sy.dma_start(out=bc3[:, :], in_=sp_pr[di, :, t0:t0 + T3]), 16)

                        @block.vector
                        def _(v):
                            s_l3.wait_all(v)
                            v.drain()
                            v.tensor_copy(bchl[0:32, :], bc3[:, :])
                            v.drain()
                            v.tensor_tensor(bcs[:, :], bc3[:, :], bchl[0:32, :], ALU.subtract)
                            v.drain()
                            s_v.inc(v.tensor_copy(bchl[32:64, :], bcs[:, :]), 1)

                        for et in range(16):
                            q_sz = et % 2

                            @block.sync
                            def _(sy, di=di, et=et, t0=t0, q_sz=q_sz):
                                s_v.wait_all(sy)
                                s_l3.inc(sy.dma_start(out=sz3[:, q_sz * T3:(q_sz + 1) * T3],
                                                      in_=sp_sz[di, et, :, t0:t0 + T3]), 16)

                            @block.vector
                            def _(v, et=et, di=di):
                                s_l3.wait_all(v)
                                v.drain()
                                v.scalar_tensor_tensor(w3[:, :], ld_sb[:, et * T3:(et + 1) * T3], -1.0,
                                                       u3_sb[:, et * T3:(et + 1) * T3], ALU.mult, ALU.mult)
                                s_v.inc(v.tensor_scalar(y3[:, :], u3_sb[:, et * T3:(et + 1) * T3],
                                                        dp_sb[:, di * 16 + et: di * 16 + et + 1], 0.0,
                                                        ALU.mult, ALU.add), 1)

                            for n in range(16):
                                q = n % 2
                                gidx = et * 16 + n

                                @block.scalar
                                def _(s, et=et, n=n, di=di, q=q):
                                    if cons_a[q]:
                                        s.wait_ge(s_v.h, cons_a[q])
                                    s_l3.wait_all(s)
                                    s_s.inc(s.activation(a3[q][:, :], ld_sb[:, et * T3:(et + 1) * T3],
                                                         ACT.Exp, bias=0.0,
                                                         scale=nA[:, di * 256 + et * 16 + n: di * 256 + et * 16 + n + 1]), 1)
                                    prod_a[q] = s_s.n

                                @block.tensor
                                def _(t, n=n, q=q):
                                    if cons_ps[q]:
                                        t.wait_ge(s_v.h, cons_ps[q])
                                    s_v.wait_all(t)
                                    for half in range(2):
                                        sl = slice(half * 512, (half + 1) * 512)
                                        t.matmul(ps_b3[q][half][:, :], sel[:, n * 128:(n + 1) * 128],
                                                 bchl[:, sl], start=True, stop=True)
                                        mm = t.matmul(ps_c3[q][half][:, :], sel[:, (16 + n) * 128:(17 + n) * 128],
                                                      bchl[:, sl], start=True, stop=True)
                                    s_t.inc(mm, 1)
                                    prod_ps[q] = s_t.n

                                @block.vector
                                def _(v, et=et, n=n, q=q, ph=ph, jj=jj, gidx=gidx,
                                      qc_prev=qc_prev, qc_new=qc_new, rev=rev):
                                    v.wait_ge(s_s.h, prod_a[q])
                                    v.wait_ge(s_t.h, prod_ps[q])
                                    for half in range(2):
                                        sl = slice(half * 512, (half + 1) * 512)
                                        v.tensor_tensor(b3[q][:, sl], w3[:, sl], ps_b3[q][half][:, :], ALU.mult)
                                    v.drain()
                                    if jj == 0:
                                        init = carry[qc_prev][:, gidx:gidx + 1] if ph == 1 else 0.0
                                    else:
                                        init = carry[qc_prev][:, gidx:gidx + 1]
                                    if not rev:
                                        v.tensor_tensor_scan(h3[q][:, :], a3[q][:, :], b3[q][:, :],
                                                             init, ALU.mult, ALU.add)
                                        v.drain()
                                        v.tensor_copy(carry[qc_new][:, gidx:gidx + 1], h3[q][:, T3 - 1:T3])
                                    else:
                                        v.tensor_tensor_scan(h3[q][:, ::-1], a3[q][:, ::-1], b3[q][:, ::-1],
                                                             init, ALU.mult, ALU.add)
                                        v.drain()
                                        v.tensor_copy(carry[qc_new][:, gidx:gidx + 1], h3[q][:, 0:1])
                                    for half in range(2):
                                        sl = slice(half * 512, (half + 1) * 512)
                                        v.tensor_tensor(tmp3[:, sl], h3[q][:, sl], ps_c3[q][half][:, :], ALU.mult)
                                    cons_a[q] = s_v.n + 1
                                    cons_ps[q] = s_v.n + 1
                                    v.drain()
                                    s_v.inc(v.tensor_tensor(y3[:, :], y3[:, :], tmp3[:, :], ALU.add), 1)

                            @block.vector
                            def _(v, et=et, q_sz=q_sz):
                                s_st.wait_all(v)
                                v.drain()
                                s_v.inc(v.tensor_tensor(yg3[:, q_sz * T3:(q_sz + 1) * T3], y3[:, :],
                                                        sz3[:, q_sz * T3:(q_sz + 1) * T3], ALU.mult), 1)

                            @block.gpsimd
                            def _(g, di=di, et=et, t0=t0, q_sz=q_sz):
                                s_v.wait_all(g)
                                s_st.inc(g.dma_start(out=sp_y[di, et, :, t0:t0 + T3],
                                                     in_=yg3[:, q_sz * T3:(q_sz + 1) * T3]), 16)

                        @block.vector
                        def _(v):
                            s_st.wait_all(v)

        if 4 in PHASES:
            s_l4 = Sem(ctx, nc, "s_l4")
            s_gl = Sem(ctx, nc, "s_gl")
            NET4 = 510
            NT4 = 9

            def tile4(i):
                n_lo = i * NET4 if i < NT4 - 1 else HALF - 16
                n_hi = min(n_lo + NET4, HALF)
                if i == NT4 - 1:
                    n_lo = (NT4 - 1) * NET4
                    n_hi = HALF
                w = n_lo - 1
                w = max(-1, min(w, HALF + 1 - 512))
                return w, n_lo, n_hi

            p4a = ExitStack()
            with p4a:
                def sba(name, shape, dt=F32):
                    return p4a.enter_context(nc.sbuf_tensor(name, shape, dt))

                wo = [sba(f"wo{d}", [128, 16 * 1024], BF16) for d in range(2)]
                wst = sba("wst", [128, 1024])
                ya = sba("ya", [128, 16 * 512], BF16)
                yb = sba("yb", [128, 16 * 512], BF16)
                oat = sba("oat", [128, 8 * 512])
                ps_o = [p4a.enter_context(nc.psum_tensor(f"ps_o{j}", [128, 512], F32)) for j in range(2)]

                for d, dn in enumerate(("a", "b")):
                    for k in range(16):
                        @block.sync
                        def _(sy, d=d, dn=dn, k=k):
                            s_s.wait_all(sy)
                            s_lw.inc(sy.dma_start(out=wst[:, :],
                                                  in_=win[dn + "_outw"][k * 128:(k + 1) * 128, :]), 16)

                        @block.scalar
                        def _(s, d=d, k=k):
                            s_lw.wait_all(s)
                            s_s.inc(s.copy(wo[d][:, k * 1024:(k + 1) * 1024], wst[:, :]), 1)

                for i in range(NT3 * (T3 // 512)):
                    t0 = i * 512

                    @block.sync
                    def _(sy, t0=t0):
                        s_t.wait_all(sy)
                        s_s.wait_all(sy)
                        s_st.wait_all(sy)
                        s_l4.inc(sy.dma_start(out=ya[:].rearrange("p (e t) -> p e t", e=16),
                                              in_=sp_y[0, :, :, t0:t0 + 512].rearrange("e p t -> p e t")), 16)
                        s_l4.inc(sy.dma_start(out=yb[:].rearrange("p (e t) -> p e t", e=16),
                                              in_=sp_y[1, :, :, t0:t0 + 512].rearrange("e p t -> p e t")), 16)

                    for md in range(8):
                        pm = ps_o[md % 2]

                        @block.tensor
                        def _(t, md=md, pm=pm):
                            s_l4.wait_all(t)
                            s_s.wait_all(t)
                            for k in range(16):
                                t.matmul(pm[:, :], wo[0][:, k * 1024 + md * 128: k * 1024 + (md + 1) * 128],
                                         ya[:, k * 512:(k + 1) * 512], start=(k == 0), stop=False)
                            for k in range(16):
                                mm = t.matmul(pm[:, :], wo[1][:, k * 1024 + md * 128: k * 1024 + (md + 1) * 128],
                                              yb[:, k * 512:(k + 1) * 512], start=False, stop=(k == 15))
                            s_t.inc(mm, 1)

                        @block.scalar
                        def _(s, md=md, pm=pm):
                            s_t.wait_all(s)
                            s_st.wait_all(s)
                            s_s.inc(s.copy(oat[:, md * 512:(md + 1) * 512], pm[:, :]), 1)

                    @block.gpsimd
                    def _(g, t0=t0):
                        s_s.wait_all(g)
                        for md in range(8):
                            s_st.inc(g.dma_start(out=sp_oat[md, :, t0:t0 + 512],
                                                 in_=oat[:, md * 512:(md + 1) * 512]), 16)

            def quant_weights(wdram, krows, wq, s_inv_col, tag):
                wstq = wq_stage
                asum = wq_asum
                for k in range(krows):
                    @block.sync
                    def _(sy, k=k):
                        s_v.wait_all(sy)
                        s_s.wait_all(sy)
                        s_st.wait_all(sy)
                        s_lw.inc(sy.dma_start(out=wstq[:, :], in_=wdram[k * 128:(k + 1) * 128, :]), 16)

                    @block.scalar
                    def _(s, k=k):
                        s_lw.wait_all(s)
                        s_s.inc(s.activation(wq_abs[:, :], wstq[:, :], ACT.Abs, bias=0.0, scale=1.0), 1)

                    @block.vector
                    def _(v, k=k):
                        s_s.wait_all(v)
                        s_v.inc(v.tensor_reduce(asum[:, k:k + 1], wq_abs[:, :], mybir.AxisListType.X, ALU.add), 1)

                    @block.gpsimd
                    def _(g, k=k):
                        s_lw.wait_all(g)
                        s_v.wait_all(g)
                        s_st.inc(g.dma_start(out=sp_wstash[k], in_=wstq[:, :]), 16)

                @block.vector
                def _(v):
                    s_v.wait_all(v)
                    v.drain()
                    s_v.inc(v.tensor_reduce(asum[:, 8:9], asum[:, 0:krows], mybir.AxisListType.X, ALU.add), 1)

                @block.tensor
                def _(t):
                    s_v.wait_all(t)
                    s_t.inc(t.matmul(ps_t4[0:1, 0:1], ones_col[:], asum[:, 8:9], start=True, stop=True), 1)

                @block.vector
                def _(v):
                    s_t.wait_all(v)
                    s_v.inc(v.tensor_copy(asum[0:1, 12:13], ps_t4[0:1, 0:1]), 1)

                @block.tensor
                def _(t):
                    s_v.wait_all(t)
                    s_t.inc(t.matmul(ps_t4[:, 1:2], ones_row[:], asum[0:1, 12:13], start=True, stop=True), 1)

                @block.vector
                def _(v):
                    s_t.wait_all(v)
                    v.tensor_copy(asum[:, 9:10], ps_t4[:, 1:2])
                    v.drain()
                    v.tensor_scalar(asum[:, 10:11], asum[:, 9:10], 1.0 / (DH * D), 1e-5, ALU.mult, ALU.max)
                    v.drain()
                    v.tensor_copy(s_inv_col, asum[:, 10:11])
                    v.drain()
                    s_v.inc(v.reciprocal(asum[:, 11:12], asum[:, 10:11]), 1)

                for k in range(krows):
                    @block.sync
                    def _(sy, k=k):
                        s_v.wait_all(sy)
                        s_s.wait_all(sy)
                        s_st.wait_all(sy)
                        s_l4.inc(sy.dma_start(out=wstq[:, :], in_=sp_wstash[k]), 16)

                    @block.vector
                    def _(v, k=k):
                        s_l4.wait_all(v)
                        v.tensor_scalar(wq_abs[:, :], wstq[:, :], asum[:, 11:12], 0.0, ALU.mult, ALU.add)
                        v.drain()
                        v.tensor_scalar(wq_abs[:, :], wq_abs[:, :], RND_MAGIC, RND_MAGIC, ALU.add, ALU.subtract)
                        v.drain()
                        s_v.inc(v.tensor_scalar(wq[:, k * 4096:(k + 1) * 4096] if tag == 1 else
                                                wq[:, k * 1024:(k + 1) * 1024],
                                                wq_abs[:, :], 1.0, -1.0, ALU.min, ALU.max), 1)

            p4b = ExitStack()
            with p4b:
                def sbb(name, shape, dt=F32):
                    return p4b.enter_context(nc.sbuf_tensor(name, shape, dt))

                wq1 = sbb("wq1", [128, 8 * 4096], BF16)
                wq_asum = sbb("wq_asum", [128, 16])
                sw1 = sbb("sw1", [128, 1])
                xn4 = sbb("xn4", [128, 8 * 512])
                xq4 = sbb("xq4", [128, 8 * 512], BF16)
                h1 = sbb("h1", [128, 32 * 512])
                wq_stage = h1[:, 0:4096]
                wq_abs = h1[:, 4096:8192]
                q2s = sbb("q2s", [128, 2 * 512], BF16)
                mx = sbb("mx", [128, 512])
                mn = sbb("mn", [128, 512])
                cv4 = sbb("cv4", [128, 512])
                srow = sbb("srow", [1, 4 * 512])
                einv = sbb("einv", [128, 512])
                ps_q = [p4b.enter_context(nc.psum_tensor(f"ps_q{j}", [128, 512], F32)) for j in range(2)]
                ps_t4 = p4b.enter_context(nc.psum_tensor("ps_t4", [128, 512], F32))

                quant_weights(w1t, 8, wq1, sw1[:, 0:1], 1)

                for i in range(NT4):
                    w, n_lo, n_hi = tile4(i)
                    c_lo = n_lo - w
                    c_hi = n_hi - w
                    nw4 = c_hi - c_lo

                    @block.vector
                    def _(v):
                        s_st.wait_all(v)

                    @block.scalar
                    def _(s):
                        s_st.wait_all(s)

                    @block.sync
                    def _(sy, w=w):
                        s_v.wait_all(sy)
                        s_s.wait_all(sy)
                        s_t.wait_all(sy)
                        for db in range(8):
                            s_l4.inc(sy.dma_start(out=xn4[:, db * 512:(db + 1) * 512],
                                                  in_=sp_xn[db, :, w + 3:w + 515]), 16)

                    @block.vector
                    def _(v):
                        s_l4.wait_all(v)
                        v.drain()
                        v.tensor_tensor(mx[:, :], xn4[:, 0 * 512:1 * 512], xn4[:, 1 * 512:2 * 512], ALU.max)
                        v.tensor_tensor(mn[:, :], xn4[:, 0 * 512:1 * 512], xn4[:, 1 * 512:2 * 512], ALU.min)
                        for db in range(2, 8):
                            v.drain()
                            v.tensor_tensor(mx[:, :], mx[:, :], xn4[:, db * 512:(db + 1) * 512], ALU.max)
                            v.tensor_tensor(mn[:, :], mn[:, :], xn4[:, db * 512:(db + 1) * 512], ALU.min)
                        v.drain()
                        s_v.inc(v.scalar_tensor_tensor(mx[:, :], mn[:, :], -1.0, mx[:, :], ALU.mult, ALU.max), 1)

                    @block.tensor
                    def _(t):
                        s_v.wait_all(t)
                        for tb in range(4):
                            mm = t.transpose(ps_t4[:, tb * 128:(tb + 1) * 128],
                                             mx[:, tb * 128:(tb + 1) * 128], ident[:])
                        s_t.inc(mm, 1)

                    @block.vector
                    def _(v):
                        s_t.wait_all(v)
                        for tb in range(4):
                            v.reduce_max(mx[:, tb:tb + 1],
                                         ps_t4[:, tb * 128:(tb + 1) * 128], mybir.AxisListType.X)
                        s_v.inc(v.drain(), 1)

                    @block.gpsimd
                    def _(g):
                        s_v.wait_all(g)
                        s_cd.inc(g.dma_start(out=sp_mrow[0], in_=mx[:, 0:4]), 16)
                        g.wait_ge(s_cd.h, s_cd.n)
                        s_gl.inc(g.dma_start(out=srow[0:1, 0:512].rearrange("o (c p) -> o c p", c=4),
                                             in_=sp_mrow[0:1].rearrange("o p c -> o c p"),
                                             allow_slow_non_contiguous=True), 16)
                    @block.vector
                    def _(v, i=i):
                        s_l4.wait_all(v)
                        s_gl.wait_all(v)
                        v.drain()
                        v.tensor_scalar(srow[0:1, 512:1024], srow[0:1, 0:512], 1.0, 1e-5, ALU.mult, ALU.max)
                        v.drain()
                        v.reciprocal(srow[0:1, 1024:1536], srow[0:1, 512:1024])
                        v.drain()
                        s_v.inc(v.tensor_scalar(srow[0:1, 1536:2048], srow[0:1, 1024:1536], 127.0, 0.0,
                                                ALU.mult, ALU.add), 1)

                    @block.tensor
                    def _(t):
                        s_v.wait_all(t)
                        mm = t.matmul(ps_t4[:, :], ones_row[:], srow[0:1, 1536:2048], start=True, stop=True)
                        s_t.inc(mm, 1)

                    @block.vector
                    def _(v):
                        s_t.wait_all(v)
                        for db in range(8):
                            v.tensor_tensor(xn4[:, db * 512:(db + 1) * 512], xn4[:, db * 512:(db + 1) * 512],
                                            ps_t4[:, :], ALU.mult)
                        v.drain()
                        v.tensor_scalar(xn4[:, :], xn4[:, :], RND_MAGIC, RND_MAGIC, ALU.add, ALU.subtract)
                        v.drain()
                        s_v.inc(v.tensor_scalar(xq4[:, :], xn4[:, :], 127.0, -128.0, ALU.min, ALU.max), 1)

                    @block.vector
                    def _(v):
                        v.tensor_scalar(srow[0:1, 0:512], srow[0:1, 512:1024], 1.0 / 127.0, 0.0, ALU.mult, ALU.add)
                        s_v.inc(v.drain(), 1)

                    @block.tensor
                    def _(t):
                        s_v.wait_all(t)
                        mm = t.matmul(ps_t4[:, :], ones_row[:], srow[0:1, 0:512], start=True, stop=True)
                        s_t.inc(mm, 1)

                    @block.scalar
                    def _(s):
                        s_t.wait_all(s)
                        s_s.inc(s.copy(einv[:, :], ps_t4[:, :]), 1)

                    @block.vector
                    def _(v):
                        s_s.wait_all(v)
                        v.drain()
                        s_v.inc(v.tensor_scalar(einv[:, :], einv[:, :], sw1[:, 0:1], 0.0, ALU.mult, ALU.add), 1)

                    for ht in range(32):
                        pm = ps_q[ht % 2]

                        @block.tensor
                        def _(t, ht=ht, pm=pm):
                            s_v.wait_all(t)
                            for k in range(8):
                                mm = t.matmul(pm[:, :], wq1[:, k * 4096 + ht * 128: k * 4096 + (ht + 1) * 128],
                                              xq4[:, k * 512:(k + 1) * 512], start=(k == 0), stop=(k == 7))
                            s_t.inc(mm, 1)

                        @block.vector
                        def _(v, ht=ht, pm=pm):
                            s_t.wait_all(v)
                            v.drain()
                            s_v.inc(v.tensor_tensor(h1[:, ht * 512:(ht + 1) * 512], pm[:, :], einv[:, :],
                                                    ALU.mult), 1)

                        @block.vector
                        def _(v, ht=ht, c_lo=c_lo, nw4=nw4):
                            s_s.wait_all(v)
                            v.drain()
                            base = ht * 512 + c_lo - 1
                            v.tensor_scalar(cv4[:, :nw4], h1[:, base:base + nw4],
                                            fcw_sb[:, ht * 3:ht * 3 + 1], 0.0, ALU.mult, ALU.add)
                            for kk in range(1, 3):
                                v.drain()
                                vi = v.scalar_tensor_tensor(cv4[:, :nw4], h1[:, base + kk:base + kk + nw4],
                                                            fcw_sb[:, ht * 3 + kk:ht * 3 + kk + 1],
                                                            cv4[:, :nw4], ALU.mult, ALU.add)
                            s_v.inc(vi, 1)

                        @block.scalar
                        def _(s, ht=ht, c_lo=c_lo, nw4=nw4):
                            s_v.wait_all(s)
                            s_s.inc(s.activation(h1[:, ht * 512 + c_lo:ht * 512 + c_lo + nw4], cv4[:, :nw4],
                                                 ACT.Silu, bias=fcb_sb[:, ht:ht + 1], scale=1.0), 1)

                    @block.vector
                    def _(v, c_lo=c_lo, nw4=nw4):
                        s_s.wait_all(v)
                        v.drain()
                        v.tensor_tensor(mx[:, :nw4], h1[:, 0 * 512 + c_lo:0 * 512 + c_lo + nw4],
                                        h1[:, 1 * 512 + c_lo:1 * 512 + c_lo + nw4], ALU.max)
                        v.tensor_tensor(mn[:, :nw4], h1[:, 0 * 512 + c_lo:0 * 512 + c_lo + nw4],
                                        h1[:, 1 * 512 + c_lo:1 * 512 + c_lo + nw4], ALU.min)
                        for ht in range(2, 32):
                            v.drain()
                            v.tensor_tensor(mx[:, :nw4], mx[:, :nw4],
                                            h1[:, ht * 512 + c_lo:ht * 512 + c_lo + nw4], ALU.max)
                            v.tensor_tensor(mn[:, :nw4], mn[:, :nw4],
                                            h1[:, ht * 512 + c_lo:ht * 512 + c_lo + nw4], ALU.min)
                        v.drain()
                        s_v.inc(v.scalar_tensor_tensor(mx[:, :nw4], mn[:, :nw4], -1.0, mx[:, :nw4], ALU.mult, ALU.max), 1)

                    @block.tensor
                    def _(t, nw4=nw4):
                        s_v.wait_all(t)
                        for tb in range(4):
                            mm = t.transpose(ps_t4[:, tb * 128:(tb + 1) * 128],
                                             mx[:, tb * 128:(tb + 1) * 128], ident[:])
                        s_t.inc(mm, 1)

                    @block.vector
                    def _(v):
                        s_t.wait_all(v)
                        for tb in range(4):
                            v.reduce_max(mx[:, tb:tb + 1], ps_t4[:, tb * 128:(tb + 1) * 128], mybir.AxisListType.X)
                        s_v.inc(v.drain(), 1)

                    @block.gpsimd
                    def _(g):
                        s_v.wait_all(g)
                        s_cd.inc(g.dma_start(out=sp_mrow[0], in_=mx[:, 0:4]), 16)
                        g.wait_ge(s_cd.h, s_cd.n)
                        s_gl.inc(g.dma_start(out=srow[0:1, 0:512].rearrange("o (c p) -> o c p", c=4),
                                             in_=sp_mrow[0:1].rearrange("o p c -> o c p"),
                                             allow_slow_non_contiguous=True), 16)

                    @block.vector
                    def _(v):
                        s_l4.wait_all(v)
                        s_gl.wait_all(v)
                        v.drain()
                        v.tensor_scalar(srow[0:1, 512:1024], srow[0:1, 0:512], 1.0, 1e-5, ALU.mult, ALU.max)
                        v.drain()
                        v.reciprocal(srow[0:1, 1024:1536], srow[0:1, 512:1024])
                        v.drain()
                        v.tensor_scalar(srow[0:1, 1536:2048], srow[0:1, 1024:1536], 127.0, 0.0, ALU.mult, ALU.add)
                        v.drain()
                        s_v.inc(v.tensor_scalar(srow[0:1, 0:512], srow[0:1, 512:1024], 1.0 / 127.0, 0.0,
                                                ALU.mult, ALU.add), 1)

                    @block.gpsimd
                    def _(g, n_lo=n_lo, c_lo=c_lo, nw4=nw4):
                        s_v.wait_all(g)
                        s_st.inc(g.dma_start(out=sp_s2[0:1, n_lo:n_lo + nw4],
                                             in_=srow[0:1, 0:nw4]), 16)

                    @block.tensor
                    def _(t):
                        s_v.wait_all(t)
                        mm = t.matmul(ps_t4[:, :], ones_row[:], srow[0:1, 1536:2048], start=True, stop=True)
                        s_t.inc(mm, 1)

                    for ht in range(32):
                        qq = ht % 2

                        @block.vector
                        def _(v, ht=ht, c_lo=c_lo, nw4=nw4, qq=qq):
                            s_t.wait_all(v)
                            s_st.wait_all(v)
                            v.drain()
                            v.tensor_tensor(cv4[:, :nw4],
                                            h1[:, ht * 512 + c_lo:ht * 512 + c_lo + nw4], ps_t4[:, :nw4], ALU.mult)
                            v.drain()
                            v.tensor_scalar(cv4[:, :nw4], cv4[:, :nw4],
                                            RND_MAGIC, RND_MAGIC, ALU.add, ALU.subtract)
                            v.drain()
                            s_v.inc(v.tensor_scalar(q2s[:, qq * 512:qq * 512 + nw4], cv4[:, :nw4],
                                                    127.0, -128.0, ALU.min, ALU.max), 1)

                        @block.gpsimd
                        def _(g, ht=ht, n_lo=n_lo, nw4=nw4, qq=qq):
                            s_v.wait_all(g)
                            s_st.inc(g.dma_start(out=sp_q2[ht, :, n_lo:n_lo + nw4],
                                                 in_=q2s[:, qq * 512:qq * 512 + nw4]), 16)

            p4c = ExitStack()
            with p4c:
                def sbc(name, shape, dt=F32):
                    return p4c.enter_context(nc.sbuf_tensor(name, shape, dt))

                wq2 = sbc("wq2", [128, 32 * 1024], BF16)
                wq_asum = sbc("wq_asum2", [128, 40])
                sw2 = sbc("sw2", [128, 1])
                q24 = sbc("q24", [128, 32 * 512], BF16)
                wq_stage = q24[:, 0:8192].bitcast(F32)
                wq_abs = q24[:, 8192:16384].bitcast(F32)
                xt4 = sbc("xt4", [128, 4 * 1024])
                xf4 = sbc("xf4", [128, 8 * 512])
                oat4 = sbc("oat4", [128, 8 * 512])
                o4 = sbc("o4", [128, 8 * 512])
                otok = xt4
                s2r = sbc("s2r", [1, 512])
                einv2 = sbc("einv2", [128, 512])
                ps_m2 = [p4c.enter_context(nc.psum_tensor(f"ps_m2{j}", [128, 512], F32)) for j in range(2)]
                ps_tc = p4c.enter_context(nc.psum_tensor("ps_tc", [128, 512], F32))

                def quant_weights2(wdram, krows, wq, s_inv_col):
                    for k in range(krows):
                        @block.sync
                        def _(sy, k=k):
                            s_v.wait_all(sy)
                            s_s.wait_all(sy)
                            s_st.wait_all(sy)
                            s_lw.inc(sy.dma_start(out=wq_stage[:, :1024],
                                                  in_=wdram[k * 128:(k + 1) * 128, :]), 16)

                        @block.scalar
                        def _(s, k=k):
                            s_lw.wait_all(s)
                            s_s.inc(s.activation(wq_abs[:, :1024], wq_stage[:, :1024], ACT.Abs, bias=0.0, scale=1.0), 1)

                        @block.vector
                        def _(v, k=k):
                            s_s.wait_all(v)
                            s_v.inc(v.tensor_reduce(wq_asum[:, k:k + 1], wq_abs[:, :1024], mybir.AxisListType.X, ALU.add), 1)

                        @block.gpsimd
                        def _(g, k=k):
                            s_lw.wait_all(g)
                            s_v.wait_all(g)
                            s_st.inc(g.dma_start(out=sp_wstash[k][:, :1024], in_=wq_stage[:, :1024]), 16)

                    @block.vector
                    def _(v, krows=krows):
                        s_v.wait_all(v)
                        v.drain()
                        s_v.inc(v.tensor_reduce(wq_asum[:, krows:krows + 1], wq_asum[:, 0:krows],
                                                mybir.AxisListType.X, ALU.add), 1)

                    @block.tensor
                    def _(t, krows=krows):
                        s_v.wait_all(t)
                        s_t.inc(t.matmul(ps_tc[0:1, 0:1], ones_col[:], wq_asum[:, krows:krows + 1], start=True, stop=True), 1)

                    @block.vector
                    def _(v, krows=krows):
                        s_t.wait_all(v)
                        s_v.inc(v.tensor_copy(wq_asum[0:1, krows + 4:krows + 5], ps_tc[0:1, 0:1]), 1)

                    @block.tensor
                    def _(t, krows=krows):
                        s_v.wait_all(t)
                        s_t.inc(t.matmul(ps_tc[:, 1:2], ones_row[:], wq_asum[0:1, krows + 4:krows + 5], start=True, stop=True), 1)

                    @block.vector
                    def _(v, krows=krows):
                        s_t.wait_all(v)
                        v.tensor_copy(wq_asum[:, krows + 1:krows + 2], ps_tc[:, 1:2])
                        v.drain()
                        v.tensor_scalar(wq_asum[:, krows + 2:krows + 3], wq_asum[:, krows + 1:krows + 2],
                                        1.0 / (DH * D), 1e-5, ALU.mult, ALU.max)
                        v.drain()
                        v.tensor_copy(s_inv_col, wq_asum[:, krows + 2:krows + 3])
                        v.drain()
                        s_v.inc(v.reciprocal(wq_asum[:, krows + 3:krows + 4], wq_asum[:, krows + 2:krows + 3]), 1)

                    for k in range(krows):
                        @block.sync
                        def _(sy, k=k):
                            s_v.wait_all(sy)
                            s_st.wait_all(sy)
                            s_l4.inc(sy.dma_start(out=wq_stage[:, :1024], in_=sp_wstash[k][:, :1024]), 16)

                        @block.vector
                        def _(v, k=k, krows=krows):
                            s_l4.wait_all(v)
                            v.drain()
                            v.tensor_scalar(wq_abs[:, :1024], wq_stage[:, :1024], wq_asum[:, krows + 3:krows + 4],
                                            0.0, ALU.mult, ALU.add)
                            v.drain()
                            v.tensor_scalar(wq_abs[:, :1024], wq_abs[:, :1024], RND_MAGIC, RND_MAGIC,
                                            ALU.add, ALU.subtract)
                            v.drain()
                            s_v.inc(v.tensor_scalar(wq[:, k * 1024:(k + 1) * 1024], wq_abs[:, :1024],
                                                    1.0, -1.0, ALU.min, ALU.max), 1)

                quant_weights2(w2t, 32, wq2, sw2[:, 0:1])

                for i in range(8):
                    t0 = i * 512

                    @block.sync
                    def _(sy, t0=t0):
                        s_t.wait_all(sy)
                        s_s.wait_all(sy)
                        s_v.wait_all(sy)
                        s_st.wait_all(sy)
                        s_l4.inc(sy.dma_start(out=q24[:].rearrange("p (e t) -> p e t", e=32),
                                              in_=sp_q2[:, :, t0:t0 + 512].rearrange("e p t -> p e t")), 16)
                        for bj in range(4):
                            s_l4.inc(sy.dma_start(out=xt4[:, bj * D:(bj + 1) * D],
                                                  in_=x_pad[t0 + HALO + bj * 128:t0 + HALO + (bj + 1) * 128, :]), 16)
                        for db in range(8):
                            s_l4.inc(sy.dma_start(out=oat4[:, db * 512:(db + 1) * 512],
                                                  in_=sp_oat[db, :, t0:t0 + 512]), 16)
                        s_l4.inc(sy.dma_start(out=s2r[0:1, :], in_=sp_s2[0:1, t0:t0 + 512]), 16)

                    for db in range(8):
                        @block.tensor
                        def _(t, db=db):
                            s_l4.wait_all(t)
                            s_s.wait_all(t)
                            for tb in range(4):
                                mm = t.transpose(ps_tc[:, tb * 128:(tb + 1) * 128],
                                                 xt4[:, tb * D + db * 128: tb * D + (db + 1) * 128], ident[:])
                            s_t.inc(mm, 1)

                        @block.scalar
                        def _(s, db=db):
                            s_t.wait_all(s)
                            s_s.inc(s.copy(xf4[:, db * 512:(db + 1) * 512], ps_tc[:, :]), 1)

                    @block.tensor
                    def _(t):
                        s_l4.wait_all(t)
                        s_v.wait_all(t)
                        s_s.wait_all(t)
                        mm = t.matmul(ps_tc[:, :], ones_row[:], s2r[0:1, :], start=True, stop=True)
                        s_t.inc(mm, 1)

                    @block.scalar
                    def _(s):
                        s_t.wait_all(s)
                        s_s.inc(s.copy(einv2[:, :], ps_tc[:, :]), 1)

                    @block.vector
                    def _(v):
                        s_s.wait_all(v)
                        v.drain()
                        s_v.inc(v.tensor_scalar(einv2[:, :], einv2[:, :], sw2[:, 0:1], 0.0, ALU.mult, ALU.add), 1)

                    for md in range(8):
                        pm = ps_m2[md % 2]

                        @block.tensor
                        def _(t, md=md, pm=pm):
                            s_v.wait_all(t)
                            for k in range(32):
                                mm = t.matmul(pm[:, :], wq2[:, k * 1024 + md * 128: k * 1024 + (md + 1) * 128],
                                              q24[:, k * 512:(k + 1) * 512], start=(k == 0), stop=(k == 31))
                            s_t.inc(mm, 1)

                        @block.vector
                        def _(v, md=md, pm=pm):
                            s_t.wait_all(v)
                            v.drain()
                            v.tensor_tensor(o4[:, md * 512:(md + 1) * 512], pm[:, :], einv2[:, :], ALU.mult)
                            v.drain()
                            v.tensor_tensor(o4[:, md * 512:(md + 1) * 512], o4[:, md * 512:(md + 1) * 512],
                                            xf4[:, md * 512:(md + 1) * 512], ALU.add)
                            v.drain()
                            s_v.inc(v.tensor_tensor(o4[:, md * 512:(md + 1) * 512], o4[:, md * 512:(md + 1) * 512],
                                                    oat4[:, md * 512:(md + 1) * 512], ALU.add), 1)

                    for tb in range(4):
                        for grp in range(2):
                            @block.tensor
                            def _(t, tb=tb, grp=grp):
                                s_v.wait_all(t)
                                s_s.wait_all(t)
                                for mi in range(4):
                                    md = grp * 4 + mi
                                    mm = t.transpose(ps_tc[:, mi * 128:(mi + 1) * 128],
                                                     o4[:, md * 512 + tb * 128: md * 512 + (tb + 1) * 128],
                                                     ident[:])
                                s_t.inc(mm, 1)

                            @block.scalar
                            def _(s, tb=tb, grp=grp):
                                s_t.wait_all(s)
                                s_s.inc(s.copy(otok[:, tb * 1024 + grp * 512: tb * 1024 + (grp + 1) * 512],
                                               ps_tc[:, :]), 1)

                    @block.gpsimd
                    def _(g, t0=t0):
                        s_s.wait_all(g)
                        for bj in range(4):
                            s_st.inc(g.dma_start(out=out_half[t0 + bj * 128:t0 + (bj + 1) * 128, :],
                                                 in_=otok[:, bj * D:(bj + 1) * D]), 16)

                @block.gpsimd
                def _(g):
                    s_st.wait_all(g)

        if 4 not in PHASES:
            zt = ctx.enter_context(nc.sbuf_tensor("zt", [128, D], F32))

            @block.gpsimd
            def _(g):
                s_st.wait_all(g)
                s_g.inc(g.memset(zt[:], 0.0), 1)
                g.wait_ge(s_g.h, s_g.n)
                for r in range(HALF // 128):
                    s_st.inc(g.dma_start(out=out_half[r * 128:(r + 1) * 128, :], in_=zt[:]), 16)
                s_st.wait_all(g)

    return nc


def _shard_inputs(inputs):
    x = np.asarray(inputs["L0_attn_x"])
    maps = []
    for c in range(NCORES):
        k, p = c // 2, c % 2
        xb = x[k]
        pad = np.zeros((HALF + 2 * HALO, D), np.float32)
        if p == 0:
            pad[HALO:, :] = xb[0:HALF + HALO]
        else:
            pad[HALO:, :] = xb[HALF - HALO:][::-1]
        dirs = ("fwd", "bwd") if p == 0 else ("bwd", "fwd")
        m = {"x_pad": np.ascontiguousarray(pad)}
        for dn, dref in zip(("a", "b"), dirs):
            g = lambda s: np.ascontiguousarray(np.asarray(inputs[f"L0_attn_{dref}_{s}"], np.float32))
            m[f"w{dn}_inw"] = g("in_w")
            m[f"w{dn}_conv"] = g("conv_w")
            m[f"w{dn}_convb"] = g("conv_b")
            m[f"w{dn}_xproj"] = g("xproj_w")
            m[f"w{dn}_dtw"] = g("dt_w")
            m[f"w{dn}_dtb"] = g("dt_b")
            m[f"w{dn}_alog"] = g("A_log")
            m[f"w{dn}_dp"] = g("Dp")
            m[f"w{dn}_outw"] = g("out_w")
        m["ident_in"] = np.eye(128, dtype=np.float32)
        import ml_dtypes
        selc = np.zeros((64, 32 * 128), ml_dtypes.bfloat16)
        for j in range(32):
            selc[j, j * 128:(j + 1) * 128] = 1
            selc[32 + j, j * 128:(j + 1) * 128] = 1
        m["sel_in"] = selc
        m["w1t"] = np.ascontiguousarray(np.asarray(inputs["L0_ffn_w1"], np.float32).T)
        m["w2t"] = np.ascontiguousarray(np.asarray(inputs["L0_ffn_w2"], np.float32).T)
        fc = np.asarray(inputs["L0_ffn_conv_w"], np.float32)[:, 0, :]
        m["fcw"] = np.ascontiguousarray(fc if p == 0 else fc[:, ::-1])
        m["fcb"] = np.ascontiguousarray(np.asarray(inputs["L0_ffn_conv_b"], np.float32))
        maps.append(m)
    return maps


def kernel(**inputs):
    maps = _shard_inputs(inputs)
    nc = build_nc()
    res = run_bass_kernel_spmd(nc, maps, core_ids=list(range(NCORES)))
    out = np.zeros((4, 8192, D), np.float32)
    for c in range(NCORES):
        k, p = c // 2, c % 2
        oh = res.results[c]["out_half"]
        if p == 0:
            out[k, 0:HALF] = oh
        else:
            out[k, HALF:] = oh[::-1]
    if DEBUG_SPILLS:
        kernel.debug = [dict(r) for r in (res.results[c] for c in range(NCORES))]
    return out
